# version 07 (final)
# baseline (speedup 1.0000x reference)
import contextlib

import numpy as np

import concourse.bass as bass
import concourse.mybir as mybir
from concourse.bass_utils import run_bass_kernel_spmd

F32 = mybir.dt.float32
BF16 = mybir.dt.bfloat16
AF = mybir.ActivationFunctionType
ALU = mybir.AluOpType
AX = mybir.AxisListType

S = 2048
D = 1024
H = 16
E = 14
FF = 2048
NT = S // 128
DC = D // 128
FC = FF // 128
EXP_SHIFT = -6.0

ENGINES = ("sync", "gpsimd", "tensor", "vector", "scalar")


class Sem:
    def __init__(self, nc, name):
        self.h = nc.alloc_semaphore(name)
        self.n = 0
        self.name = name


class Plan:

    def __init__(self, nc):
        self.nc = nc
        self.ops = {e: [] for e in ENGINES}

    def add(self, engine, fn, waits=(), inc=None, amt=1):
        ws = tuple(t for t in waits if t is not None)
        self.ops[engine].append((ws, fn, inc, amt))
        if inc is not None:
            inc.n += amt
            return (inc, inc.n)
        return None

    def emit(self):
        with self.nc.Block() as block:
            for name in ENGINES:
                ops = self.ops[name]
                if not ops:
                    continue

                def body(eng, ops=ops):
                    seen = {}
                    for waits, fn, inc, amt in ops:
                        for sem, val in waits:
                            if seen.get(sem.name, 0) < val:
                                eng.wait_ge(sem.h, val)
                                seen[sem.name] = val
                        inst = fn(eng)
                        if inc is not None:
                            inst.then_inc(inc.h, amt)

                getattr(block, name)(body)


def build_program(dbg=()):
    nc = bass.Bass("TRN2", target_bir_lowering=False,
                   detect_race_conditions=False,
                   dynamic_dma_scratch_size=16384)

    xd = nc.dram_tensor("x", [S, D], F32, kind="ExternalInput").ap()
    wq_d = nc.dram_tensor("Wq", [D, D], F32, kind="ExternalInput").ap()
    wk_d = nc.dram_tensor("Wk", [D, D], F32, kind="ExternalInput").ap()
    wv_d = nc.dram_tensor("Wv", [D, D], F32, kind="ExternalInput").ap()
    wo_d = nc.dram_tensor("Wo", [D, D], F32, kind="ExternalInput").ap()
    bq_d = nc.dram_tensor("bq", [D], F32, kind="ExternalInput").ap()
    bk_d = nc.dram_tensor("bk", [D], F32, kind="ExternalInput").ap()
    bv_d = nc.dram_tensor("bv", [1, D], F32, kind="ExternalInput").ap()
    bo_d = nc.dram_tensor("bo", [1, D], F32, kind="ExternalInput").ap()
    rw_d = nc.dram_tensor("rw", [D, E], F32, kind="ExternalInput").ap()
    w1_d = nc.dram_tensor("w1", [E, D, FF], F32, kind="ExternalInput").ap()
    b1_d = nc.dram_tensor("b1", [E, FF], F32, kind="ExternalInput").ap()
    w2_d = nc.dram_tensor("w2", [E, FF, D], F32, kind="ExternalInput").ap()
    b2_d = nc.dram_tensor("b2", [E, D], F32, kind="ExternalInput").ap()
    out_d = nc.dram_tensor("out", [S, D], F32, kind="ExternalOutput").ap()

    dbg_aps = {}

    def dbg_out(name, shape, dtype):
        ap = nc.dram_tensor("dbg_" + name, shape, dtype,
                            kind="ExternalOutput").ap()
        dbg_aps[name] = ap
        return ap

    p = Plan(nc)
    dS = Sem(nc, "dS")
    dS2 = Sem(nc, "dS2")
    dG = Sem(nc, "dG")
    ten = Sem(nc, "ten")
    vec = Sem(nc, "vec")
    sca = Sem(nc, "sca")
    gps = Sem(nc, "gps")

    ctx = contextlib.ExitStack()
    with ctx:
        x_sb = ctx.enter_context(nc.sbuf_tensor("x_sb", [128, NT, D], F32))
        xn_F = ctx.enter_context(nc.sbuf_tensor("xn_F", [128, DC, S], BF16))
        ident = ctx.enter_context(nc.sbuf_tensor("ident", [128, 128], BF16))
        trimT = ctx.enter_context(nc.sbuf_tensor("trimT", [128, 128], BF16))
        ones = ctx.enter_context(nc.sbuf_tensor("ones", [128, 128], BF16))
        maskw = ctx.enter_context(
            nc.sbuf_tensor("maskw", [128, NT, E + 1], BF16))
        bq8 = ctx.enter_context(nc.sbuf_tensor("bq8", [128, DC], F32))
        bk_s = ctx.enter_context(nc.sbuf_tensor("bk_s", [128, DC], F32))
        b1T = ctx.enter_context(nc.sbuf_tensor("b1T", [128, FC, E], BF16))
        eshift = ctx.enter_context(nc.sbuf_tensor("eshift", [128, 1], F32))


        p.add("gpsimd", lambda g: g.memset(ones[:], 1.0))
        p.add("gpsimd", lambda g: g.affine_select(
            ident[:], ones[:], [[1, 128]], ALU.is_equal, 0.0,
            channel_multiplier=-1))
        t_tri = p.add("gpsimd", lambda g: g.affine_select(
            trimT[:], ones[:], [[1, 128]], ALU.is_ge, 0.0,
            channel_multiplier=-1), inc=gps)
        t_esh = p.add("gpsimd", lambda g: g.memset(eshift[:], EXP_SHIFT),
                      inc=gps)

        t_x = []
        for i in range(4):
            eng, sem = (("sync", dS) if i % 2 == 0 else ("scalar", dS2))
            t_x.append(p.add(
                eng,
                lambda s, i=i: s.dma_start(
                    x_sb[:, 4 * i:4 * i + 4, :],
                    xd.rearrange("(t p) d -> p t d", p=128)[:, 4 * i:4 * i + 4, :]),
                inc=sem, amt=16))

        t_bq = p.add("sync", lambda s: s.dma_start(
            bq8[:], bass.AP(bq_d.tensor, 0, [[1, 128], [128, DC]]),
            allow_slow_non_contiguous=True), inc=dS, amt=16)
        t_bk = p.add("sync", lambda s: s.dma_start(
            bk_s[:], bass.AP(bk_d.tensor, 0, [[1, 128], [128, DC]]),
            allow_slow_non_contiguous=True), inc=dS, amt=16)
        t_bq8 = p.add("vector", lambda v: v.tensor_scalar_mul(
            bq8[:], bq8[:], 0.125), waits=[t_bq], inc=vec)

        wsrc = [wk_d, wv_d, wq_d, wo_d]
        n_wld = 0
        wld_tokens = {}

        def plan_wload(war_toks):
            nonlocal n_wld
            i = n_wld
            mi, oc = i // DC, i % DC
            buf = i % 2
            n_wld += 1
            tok = p.add("gpsimd", lambda g, mi=mi, oc=oc, buf=buf: g.dma_start(
                wtmp[:, buf, :],
                wsrc[mi].rearrange("(c p) i -> p c i", p=128)[:, oc, :]),
                waits=list(war_toks), inc=dG, amt=16)
            wld_tokens[(mi, oc)] = (tok, buf)
            return tok


        with contextlib.ExitStack() as ph1:
            xn_t = ph1.enter_context(nc.sbuf_tensor("xn_t", [128, 4, D], F32))
            hi_t = ph1.enter_context(nc.sbuf_tensor("hi_t", [128, 4, D], BF16))
            lo_t = ph1.enter_context(nc.sbuf_tensor("lo_t", [128, 4, D], BF16))
            lo_F = ph1.enter_context(
                nc.sbuf_tensor("lo_F", [128, 4, DC, 128], BF16))
            bnst = ph1.enter_context(
                nc.sbuf_tensor("bnst", [128, 4, 12], F32))
            st1 = ph1.enter_context(nc.sbuf_tensor("st1", [128, NT, 8], F32))
            logit = ph1.enter_context(
                nc.sbuf_tensor("logit", [128, NT, E], F32))
            expl = ph1.enter_context(nc.sbuf_tensor("expl", [128, 4, E], F32))
            rw_f = ph1.enter_context(nc.sbuf_tensor("rw_f", [128, DC, E], F32))
            rhl = ph1.enter_context(
                nc.sbuf_tensor("rhl", [128, DC, 2 * E], BF16))
            b1n = ph1.enter_context(nc.sbuf_tensor("b1n", [E, FF], BF16))

            trPh = ph1.enter_context(
                nc.psum_tensor("trPh", [128, DC, 128], BF16))
            trPl = ph1.enter_context(
                nc.psum_tensor("trPl", [128, DC, 128], BF16))
            rp1 = ph1.enter_context(nc.psum_tensor("rp1", [128, E], F32))
            b1tP = ph1.enter_context(
                nc.psum_tensor("b1tP", [128, FC, E], BF16))

            t_rw = p.add("sync", lambda s: s.dma_start(
                rw_f[:], rw_d.rearrange("(c p) e -> p c e", p=128),
                allow_slow_non_contiguous=True), inc=dS, amt=16)
            p.add("vector", lambda v: v.tensor_copy(rhl[:, :, 0:E], rw_f[:]),
                  waits=[t_rw])
            for _ in range(3):
                p.add("vector", lambda v: v.engine_nop())
            t_rlo = p.add("vector", lambda v: v.tensor_tensor(
                rhl[:, :, E:2 * E], rw_f[:], rhl[:, :, 0:E], ALU.subtract),
                inc=vec)
            t_b1 = p.add("gpsimd", lambda g: g.dma_start(b1n[:], b1_d),
                         inc=dG, amt=16)

            def b1tr(te):
                for j in range(FC):
                    inst = te.transpose(
                        b1tP[:, j, :], b1n[0:E, 128 * j:128 * j + 128],
                        ident[0:E, 0:E])
                return inst
            t_b1t = p.add("tensor", b1tr, waits=[t_b1, t_tri], inc=ten)
            t_b1T = p.add("vector", lambda v: v.tensor_copy(b1T[:], b1tP[:]),
                          waits=[t_b1t], inc=vec)

            tok_xn = [None] * NT
            tok_route = [None] * NT
            tok_loF = [None] * NT
            trtok = [None] * NT
            last1_sca = None
            for t4 in range(NT // 4):
                tiles = [4 * t4 + i for i in range(4)]
                for t in tiles:
                    p.add("vector", lambda v, t=t: v.bn_stats(
                        bnst[:, t % 4, 0:6], x_sb[:, t, 0:512]),
                        waits=[t_x[t // 4]])
                for t in tiles:
                    p.add("vector", lambda v, t=t: v.bn_stats(
                        bnst[:, t % 4, 6:12], x_sb[:, t, 512:1024]))
                for t in tiles:
                    p.add("vector", lambda v, t=t: v.bn_aggr(
                        st1[:, t, 2:4], bnst[:, t % 4, :]))
                for t in tiles:
                    p.add("vector", lambda v, t=t: v.tensor_scalar_add(
                        st1[:, t, 3:4], st1[:, t, 3:4], 1e-5))
                t_rcs = {}
                for t in tiles:
                    t_rcs[t] = p.add("vector", lambda v, t=t: v.reciprocal(
                        st1[:, t, 5:6], st1[:, t, 3:4]), inc=vec)
                t_rss = {}
                for t in tiles:
                    t_rss[t] = p.add("scalar", lambda s, t=t: s.activation(
                        st1[:, t, 6:7], st1[:, t, 5:6], AF.Sqrt),
                        waits=[t_rcs[t]], inc=sca)
                t_bls = {}
                for t in tiles:
                    t_bls[t] = p.add(
                        "vector", lambda v, t=t: v.scalar_tensor_tensor(
                            st1[:, t, 7:8], st1[:, t, 2:3], -1.0,
                            st1[:, t, 6:7], ALU.mult, ALU.mult),
                        waits=[t_rss[t]], inc=vec)
                t_xns = {}
                for t in tiles:
                    t_xns[t] = p.add(
                        "scalar", lambda s, t=t: s.activation(
                            xn_t[:, t % 4, :], x_sb[:, t, :], AF.Identity,
                            bias=st1[:, t, 7:8], scale=st1[:, t, 6:7]),
                        waits=[t_bls[t]], inc=sca)
                    last1_sca = t_xns[t]
                t_his = {}
                for t in tiles:
                    w_hi = [t_xns[t]]
                    if t >= 4:
                        w_hi.append(trtok[t - 4])
                    t_his[t] = p.add("vector", lambda v, t=t: v.tensor_copy(
                        hi_t[:, t % 4, :], xn_t[:, t % 4, :]),
                        waits=w_hi, inc=vec)
                t_los = {}
                for t in tiles:
                    t_los[t] = p.add("vector", lambda v, t=t: v.tensor_tensor(
                        lo_t[:, t % 4, :], xn_t[:, t % 4, :],
                        hi_t[:, t % 4, :], ALU.subtract), inc=vec)

                for t in tiles:
                    b = t % 4
                    w_tr = [t_his[t], t_los[t], t_tri]
                    if t >= 4:
                        w_tr += [tok_xn[t - 4], tok_loF[t - 4]]

                    def tr_hi(te, b=b):
                        for c in range(DC):
                            inst = te.transpose(
                                trPh[:, c, :],
                                hi_t[:, b, 128 * c:128 * c + 128], ident[:])
                        return inst
                    t_trh = p.add("tensor", tr_hi, waits=w_tr, inc=ten)

                    def tr_lo(te, b=b):
                        for c in range(DC):
                            inst = te.transpose(
                                trPl[:, c, :],
                                lo_t[:, b, 128 * c:128 * c + 128], ident[:])
                        return inst
                    t_trl = p.add("tensor", tr_lo, inc=ten)
                    trtok[t] = t_trl
                    tok_xn[t] = p.add("vector", lambda v, t=t: v.tensor_copy(
                        bass.AP(xn_F, 128 * t,
                                [[DC * S, 128], [S, DC], [1, 128]]),
                        trPh[:]), waits=[t_trh], inc=vec)
                    tok_loF[t] = p.add("vector", lambda v, b=b: v.tensor_copy(
                        lo_F[:, b, :, :], trPl[:]), waits=[t_trl],
                        inc=vec)

                    w_rt = [tok_xn[t], tok_loF[t], t_rlo]
                    if t >= 1:
                        w_rt.append(tok_route[t - 1])

                    def rmm(te, t=t, b=b):
                        for half in range(2):
                            for c in range(DC):
                                te.matmul(
                                    rp1[:],
                                    xn_F[:, c, 128 * t:128 * t + 128],
                                    rhl[:, c, E * half:E * half + E],
                                    start=(half == 0 and c == 0), stop=False)
                        for half in range(2):
                            for c in range(DC):
                                inst = te.matmul(
                                    rp1[:], lo_F[:, b, c, :],
                                    rhl[:, c, E * half:E * half + E],
                                    start=False,
                                    stop=(half == 1 and c == DC - 1))
                        return inst
                    t_rmm = p.add("tensor", rmm, waits=w_rt, inc=ten)
                    tok_route[t] = p.add(
                        "vector", lambda v, t=t: v.tensor_copy(
                            logit[:, t, :], rp1[:]),
                        waits=[t_rmm], inc=vec)

            t_mwone = p.add("vector", lambda v: v.memset(
                bass.AP(maskw, E, [[NT * (E + 1), 128], [E + 1, NT]]), 1.0),
                inc=vec)
            t_mw = None
            for t4 in range(NT // 4):
                tiles = [4 * t4 + i for i in range(4)]
                for t in tiles:
                    p.add("vector", lambda v, t=t: v.reduce_max(
                        st1[:, t, 0:1], logit[:, t, :], axis=AX.X),
                        waits=[tok_route[t]])
                t_nms = {}
                for t in tiles:
                    t_nms[t] = p.add("vector", lambda v, t=t: v.tensor_scalar_mul(
                        st1[:, t, 1:2], st1[:, t, 0:1], -1.0), inc=vec)
                t_exs = {}
                for t in tiles:
                    t_exs[t] = p.add("scalar", lambda s, t=t: s.activation(
                        expl[:, t % 4, :], logit[:, t, :], AF.Exp,
                        bias=st1[:, t, 1:2]), waits=[t_nms[t]], inc=sca)
                for t in tiles:
                    p.add("vector", lambda v, t=t: v.reduce_sum(
                        st1[:, t, 2:3], expl[:, t % 4, :], axis=AX.X),
                        waits=[t_exs[t]])
                for t in tiles:
                    p.add("vector", lambda v, t=t: v.reciprocal(
                        st1[:, t, 3:4], st1[:, t, 2:3]))
                for t in tiles:
                    t_mw = p.add("vector", lambda v, t=t: v.tensor_scalar(
                        maskw[:, t, 0:E], logit[:, t, :], st1[:, t, 0:1],
                        st1[:, t, 3:4], ALU.is_equal, ALU.mult), inc=vec)

            dbg_toks1 = []
            if "logits" in dbg:
                ap = dbg_out("logits", [128, NT, E], F32)
                dbg_toks1.append(p.add(
                    "sync", lambda s, ap=ap: s.dma_start(ap, logit[:]),
                    waits=[t_mw], inc=dS, amt=16))
            if "maskw" in dbg:
                ap = dbg_out("maskw", [128, NT, E], BF16)
                dbg_toks1.append(p.add(
                    "sync", lambda s, ap=ap: s.dma_start(
                        ap, maskw[:, :, 0:E]), waits=[t_mw], inc=dS, amt=16))
            if "xnf" in dbg:
                ap = dbg_out("xnf", [128, DC, S], BF16)
                dbg_toks1.append(p.add(
                    "sync", lambda s, ap=ap: s.dma_start(ap, xn_F[:]),
                    waits=[tok_xn[NT - 1]], inc=dS, amt=16))
            if dbg_toks1:
                t_dbg1 = p.add("vector", lambda v: v.engine_nop(),
                               waits=dbg_toks1, inc=vec)
            else:
                t_dbg1 = None

            ph1_done = [t_mw, last1_sca, t_rmm, t_b1T, t_dbg1]

        with contextlib.ExitStack() as ph2:
            wqo = ph2.enter_context(
                nc.sbuf_tensor("wqo", [128, 2 * DC, D], BF16))
            k_F = ph2.enter_context(nc.sbuf_tensor("k_F", [128, DC, S], BF16))
            v_tm = ph2.enter_context(
                nc.sbuf_tensor("v_tm", [128, NT, 65 * H], BF16))

            t_bv = p.add("gpsimd", lambda g: g.dma_start(
                wqo[0:1, 0, :], bv_d), waits=ph1_done, inc=dG, amt=16)

            with contextlib.ExitStack() as ph2w:
                wtmp = ph2w.enter_context(
                    nc.sbuf_tensor("wtmp", [128, 2, D], BF16))
                wkv = ph2w.enter_context(
                    nc.sbuf_tensor("wkv", [128, DC, 512], BF16))
                plan_wload(ph1_done)
                plan_wload(ph1_done)
                wtP = ph2w.enter_context(
                    nc.psum_tensor("wtP", [128, DC, 128], BF16))
                qkvP = [ph2w.enter_context(
                    nc.psum_tensor(f"qkvP{i}", [128, 512], F32))
                    for i in range(2)]

                def prep_WT(mi, dst_tensor, oc_base, dst_nc, extra_wait):
                    toks = []
                    t_tr = None
                    for oc in range(DC):
                        t_ld, buf = wld_tokens[(mi, oc)]
                        def wtr(te, buf=buf):
                            for ic in range(DC):
                                inst = te.transpose(
                                    wtP[:, ic, :],
                                    wtmp[:, buf, 128 * ic:128 * ic + 128],
                                    ident[:])
                            return inst
                        w_tr = [t_ld] + list(extra_wait)
                        if toks:
                            w_tr.append(toks[-1])
                        t_tr = p.add("tensor", wtr, waits=w_tr, inc=ten)
                        toks.append(p.add(
                            "vector",
                            lambda v, oc=oc: v.tensor_copy(
                                bass.AP(dst_tensor,
                                        oc_base * D + 128 * oc,
                                        [[dst_nc * D, 128], [D, DC],
                                         [1, 128]]),
                                wtP[:]),
                            waits=[t_tr], inc=vec))
                        if n_wld < 4 * DC:
                            plan_wload([t_tr])
                    return toks[-1], t_tr

                def prep_half(mi, half, extra_wait):
                    toks = []
                    t_tr = None
                    for oc in range(4 * half, 4 * half + 4):
                        t_ld, buf = wld_tokens[(mi, oc)]

                        def wtr(te, buf=buf):
                            for ic in range(DC):
                                inst = te.transpose(
                                    wtP[:, ic, :],
                                    wtmp[:, buf, 128 * ic:128 * ic + 128],
                                    ident[:])
                            return inst
                        w_tr = [t_ld] + list(extra_wait)
                        if toks:
                            w_tr.append(toks[-1])
                        t_tr = p.add("tensor", wtr, waits=w_tr, inc=ten)
                        toks.append(p.add(
                            "vector",
                            lambda v, oc=oc, half=half: v.tensor_copy(
                                bass.AP(wkv, 128 * (oc - 4 * half),
                                        [[DC * 512, 128], [512, DC],
                                         [1, 128]]),
                                wtP[:]),
                            waits=[t_tr], inc=vec))
                        if n_wld < 4 * DC:
                            plan_wload([t_tr])
                    return toks[-1], t_tr

                tok_k = []
                t_kmm = None
                for half in range(2):
                    ew = list(ph1_done) if half == 0 else [t_kmm]
                    t_wkT, _ = prep_half(0, half, ew)
                    for oc4 in range(4):
                        oc = 4 * half + oc4
                        for j4 in range(4):
                            buf = (oc * 4 + j4) % 2
                            w = [t_wkT, tok_xn[NT - 1], t_bk]
                            if len(tok_k) >= 2:
                                w.append(tok_k[-2])

                            def kmm(te, oc4=oc4, j4=j4, buf=buf):
                                for ic in range(DC):
                                    inst = te.matmul(
                                        qkvP[buf][:],
                                        wkv[:, ic, 128 * oc4:128 * oc4 + 128],
                                        xn_F[:, ic, 512 * j4:512 * j4 + 512],
                                        start=(ic == 0), stop=(ic == DC - 1))
                                return inst
                            t_kmm = p.add("tensor", kmm, waits=w, inc=ten)
                            tok_k.append(p.add(
                                "scalar",
                                lambda s, oc=oc, j4=j4, buf=buf: s.activation(
                                    k_F[:, oc, 512 * j4:512 * j4 + 512],
                                    qkvP[buf][:], AF.Identity,
                                    bias=bk_s[:, oc:oc + 1]),
                                waits=[t_kmm], inc=sca))
                t_kdone = tok_k[-1]

                t_ones_v = p.add("vector", lambda v: v.memset(
                    bass.AP(v_tm, 64,
                            [[NT * 65 * H, 128], [65 * H, NT], [65, H]]),
                    1.0), inc=vec)
                tok_v = []
                t_vmm = None
                for half in range(2):
                    oc2 = half
                    t_wvT, _ = prep_half(1, half, [t_kmm if half == 0 else t_vmm])
                    for st in range(NT):
                        buf = (st + 2 * half) % 2
                        w = [t_wvT, t_bv, tok_xn[NT - 1]]
                        if len(tok_v) >= 2:
                            w.append(tok_v[-2])
                        else:
                            w.append(tok_k[-1])

                        def vmm(te, st=st, oc2=oc2, buf=buf):
                            for ic in range(DC):
                                te.matmul(
                                    qkvP[buf][:],
                                    xn_F[:, ic, 128 * st:128 * st + 128],
                                    wkv[:, ic, :],
                                    start=(ic == 0), stop=False)
                            return te.matmul(
                                qkvP[buf][:], ones[0:1, :],
                                wqo[0:1, 0, 512 * oc2:512 * oc2 + 512],
                                start=False, stop=True)
                        t_vmm = p.add("tensor", vmm, waits=w, inc=ten)
                        tok_v.append(p.add(
                            "vector",
                            lambda v, st=st, oc2=oc2, buf=buf: v.tensor_copy(
                                bass.AP(v_tm, st * 65 * H + oc2 * 65 * 8,
                                        [[NT * 65 * H, 128], [65, 8],
                                         [1, 64]]),
                                qkvP[buf][:]),
                            waits=[t_vmm], inc=vec))
                t_vdone = tok_v[-1]

                t_wqT, _ = prep_WT(2, wqo, 0, 2 * DC, [t_vmm])
                t_woT, t_wo_tr = prep_WT(3, wqo, DC, 2 * DC, [t_vmm, t_wqT])
                ph2w_done = [t_kdone, t_vdone, t_wqT, t_woT, t_wo_tr,
                             tok_v[-2]]

            with contextlib.ExitStack() as ph2b:
                q_g = ph2b.enter_context(
                    nc.sbuf_tensor("q_g", [128, DC, 128], BF16))
                attnT = ph2b.enter_context(
                    nc.sbuf_tensor("attnT", [128, 2, NT, 128], BF16))
                heads = ph2b.enter_context(
                    nc.sbuf_tensor("heads", [128, D], BF16))
                inv_s = ph2b.enter_context(
                    nc.sbuf_tensor("inv_s", [128, H], F32))
                rs_t = ph2b.enter_context(
                    nc.sbuf_tensor("rs_t", [128, H], F32))

                scP = [ph2b.enter_context(
                    nc.psum_tensor(f"scP{i}", [128, 8, 128], F32))
                    for i in range(2)]
                avP = ph2b.enter_context(
                    nc.psum_tensor("avP", [128, 65], F32))
                trP = ph2b.enter_context(
                    nc.psum_tensor("trPa", [128, DC, 128], BF16))
                woP = ph2b.enter_context(
                    nc.psum_tensor("woP", [128, 2, 512], F32))

                q_last_evac = [None]

                def emit_qgroup(qt, w_extra):
                    toks = []
                    for oc in range(DC):
                        w = [t_wqT, tok_xn[NT - 1], t_bq8] + list(w_extra)
                        if toks:
                            w = [toks[-1]]
                        elif q_last_evac[0] is not None:
                            w.append(q_last_evac[0])

                        def qmm(te, oc=oc, qt=qt):
                            for ic in range(DC):
                                inst = te.matmul(
                                    bass.AP(woP, 0, [[1024, 128], [1, 128]]),
                                    wqo[:, ic, 128 * oc:128 * oc + 128],
                                    xn_F[:, ic, 128 * qt:128 * qt + 128],
                                    start=(ic == 0), stop=(ic == DC - 1))
                            return inst
                        t_mmq = p.add("tensor", qmm, waits=w, inc=ten)
                        toks.append(p.add(
                            "scalar",
                            lambda s, oc=oc: s.activation(
                                q_g[:, oc, :],
                                bass.AP(woP, 0, [[1024, 128], [1, 128]]),
                                AF.Identity,
                                bias=bq8[:, oc:oc + 1], scale=0.125),
                            waits=[t_mmq], inc=sca))
                    q_last_evac[0] = toks[-1]
                    return toks[-1]

                tok_q = emit_qgroup(0, ph2w_done)

                sc_cons = [None, None]
                av_cons = [None, None]
                at_cons = [None, None]
                heads_cons = None
                headsF_cons = None
                grp_ctr = 0
                t_out_tok = [None] * NT
                last_scores = None

                def emit_av(pt, h, t_msk):
                    nch = pt + 1
                    ab = h % 2
                    w_av = [t_msk]
                    if av_cons[0] is not None:
                        w_av.append(av_cons[0])

                    def avmm(te, h=h, nch=nch, ab=ab):
                        for c2 in range(nch):
                            inst = te.matmul(
                                avP[:], attnT[:, ab, c2, :],
                                v_tm[:, c2, 65 * h:65 * h + 65],
                                start=(c2 == 0), stop=(c2 == nch - 1))
                        return inst
                    t_avmm = p.add("tensor", avmm, waits=w_av, inc=ten)
                    at_cons[ab] = t_avmm
                    w_ev = [t_avmm]
                    if h == 0 and headsF_cons[0] is not None:
                        w_ev.append(headsF_cons[0])
                    p.add("vector", lambda v, h=h: v.tensor_copy(
                        rs_t[:, h:h + 1], avP[:, 64:65]),
                        waits=w_ev)
                    av_cons[0] = p.add(
                        "vector",
                        lambda v, h=h: v.tensor_copy(
                            heads[:, 64 * h:64 * h + 64],
                            avP[:, 0:64]),
                        inc=vec)
                    return t_avmm

                def emit_tile_end(t):
                    p.add("vector", lambda v: v.engine_nop())
                    p.add("vector", lambda v: v.engine_nop())
                    p.add("vector", lambda v: v.reciprocal(
                        inv_s[:], rs_t[:]))
                    p.add("vector", lambda v: v.engine_nop())
                    p.add("vector", lambda v: v.engine_nop())
                    p.add("vector", lambda v: v.engine_nop())
                    t_heads = p.add(
                        "vector", lambda v: v.tensor_tensor(
                            heads[:], heads[:],
                            bass.AP(inv_s, 0, [[H, 128], [1, H], [0, 64]]),
                            ALU.mult),
                        inc=vec)
                    if t == 0 and "heads0" in dbg:
                        ap = dbg_out("heads0", [128, D], BF16)
                        tk = p.add("sync", lambda s, ap=ap: s.dma_start(
                            ap, heads[:]), waits=[t_heads], inc=dS, amt=16)
                        t_heads = p.add("vector", lambda v: v.engine_nop(),
                                        waits=[tk, t_heads], inc=vec)
                    w_tr = [t_heads]
                    if headsF_cons[0] is not None:
                        w_tr.append(headsF_cons[0])

                    def htr(te):
                        for oc in range(DC):
                            inst = te.transpose(
                                trP[:, oc, :],
                                heads[:, 128 * oc:128 * oc + 128], ident[:])
                        return inst
                    heads_cons[0] = p.add("tensor", htr, waits=w_tr, inc=ten)
                    t_hF = p.add("vector", lambda v: v.tensor_copy(
                        heads[:], trP[:]), waits=[heads_cons[0]], inc=vec)

                    w_wo = [t_hF, t_woT]
                    if q_last_evac[0] is not None:
                        w_wo.append(q_last_evac[0])

                    def womm(te):
                        for oc in range(DC):
                            for j in range(2):
                                inst = te.matmul(
                                    woP[:, j, :],
                                    heads[:, 128 * oc:128 * oc + 128],
                                    wqo[:, DC + oc, 512 * j:512 * j + 512],
                                    start=(oc == 0), stop=(oc == DC - 1))
                        return inst
                    headsF_cons[0] = p.add("tensor", womm, waits=w_wo,
                                           inc=ten)
                    t_out_tok[t] = p.add(
                        "vector",
                        lambda v, t=t: v.tensor_add(
                            x_sb[:, t, :], x_sb[:, t, :],
                            bass.AP(woP, 0, [[1024, 128], [1, 1024]])),
                        waits=[headsF_cons[0]], inc=vec)

                heads_cons = [None]
                headsF_cons = [None]
                pend = None
                pend_end = None
                for t in range(NT):
                    g2 = t // 2
                    nch = t + 1
                    for h in range(H):
                        poff = 64 * (h % 2)
                        kc = h // 2
                        ngrp = (nch + 7) // 8
                        smm_toks = []
                        for gg in range(ngrp):
                            b = grp_ctr % 2
                            grp_ctr += 1
                            c0, c1 = 8 * gg, min(8 * gg + 8, nch)
                            w = [tok_q]
                            if sc_cons[b] is not None:
                                w.append(sc_cons[b])

                            def smm(te, b=b, c0=c0, c1=c1,
                                    poff=poff, kc=kc):
                                for c2 in range(c0, c1):
                                    inst = te.matmul(
                                        scP[b][:, c2 - c0, :],
                                        k_F[poff:poff + 64, kc,
                                            128 * c2:128 * c2 + 128],
                                        q_g[poff:poff + 64, kc, :],
                                        start=True, stop=True)
                                return inst
                            t_smm = p.add("tensor", smm, waits=w, inc=ten)
                            last_scores = t_smm
                            smm_toks.append((b, c0, c1, t_smm))

                        if pend_end is not None:
                            emit_tile_end(pend_end)
                            pend_end = None
                        if pend is not None:
                            emit_av(*pend)
                            if pend[1] == H - 1:
                                pend_end = pend[0]
                            pend = None

                        ab = h % 2
                        last_exp = None
                        for b, c0, c1, t_smm in smm_toks:
                            w_e = [t_smm]
                            if at_cons[ab] is not None:
                                w_e.append(at_cons[ab])

                            def eop(s, ab=ab, b=b, c0=c0, c1=c1):
                                n = c1 - c0
                                return s.activation(
                                    bass.AP(attnT,
                                            ab * NT * 128 + c0 * 128,
                                            [[2 * NT * 128, 128],
                                             [1, n * 128]]),
                                    bass.AP(scP[b], 0,
                                            [[8 * 128, 128], [1, n * 128]]),
                                    AF.Exp, bias=eshift[:])
                            sc_cons[b] = p.add("scalar", eop,
                                               waits=w_e + [t_esh], inc=sca)
                            last_exp = sc_cons[b]
                        t_msk = p.add(
                            "vector", lambda v, t=t, ab=ab: v.tensor_mul(
                                attnT[:, ab, t, :], attnT[:, ab, t, :],
                                trimT[:]),
                            waits=[last_exp], inc=vec)
                        if t == 0 and h == 0 and "attnT00" in dbg:
                            ap = dbg_out("attnT00", [128, 128], BF16)
                            tk = p.add("sync",
                                       lambda s, ap=ap: s.dma_start(
                                           ap, attnT[:, 0, 0, :]),
                                       waits=[t_msk], inc=dS, amt=16)
                            t_msk = p.add("vector",
                                          lambda v: v.engine_nop(),
                                          waits=[tk, t_msk], inc=vec)
                        pend = (t, h, t_msk)

                    if t < NT - 1:
                        w_q = [tk for tk in (t_out_tok[t - 1] if t >= 1
                                             else None,) if tk]
                        tok_q = emit_qgroup(t + 1, w_q)

                if pend_end is not None:
                    emit_tile_end(pend_end)
                emit_av(*pend)
                emit_tile_end(NT - 1)
                heads_cons = heads_cons[0]
                headsF_cons = headsF_cons[0]

                dbg_toks2 = []
                dbg_att = {}
                if "attn_out" in dbg:
                    ap = dbg_out("attn_out", [128, NT, D], F32)
                    dbg_toks2.append(p.add(
                        "sync", lambda s, ap=ap: s.dma_start(ap, x_sb[:]),
                        waits=[t_out_tok[NT - 1]], inc=dS, amt=16))
                if "kf" in dbg:
                    ap = dbg_out("kf", [128, DC, S], BF16)
                    dbg_toks2.append(p.add(
                        "sync", lambda s, ap=ap: s.dma_start(ap, k_F[:]),
                        waits=[t_kdone], inc=dS, amt=16))
                if "vtm" in dbg:
                    ap = dbg_out("vtm", [128, NT, 65 * H], BF16)
                    dbg_toks2.append(p.add(
                        "sync", lambda s, ap=ap: s.dma_start(ap, v_tm[:]),
                        waits=[t_vdone, t_ones_v], inc=dS, amt=16))
                if "qg" in dbg:
                    ap = dbg_out("qg", [128, DC, 128], BF16)
                    dbg_toks2.append(p.add(
                        "sync", lambda s, ap=ap: s.dma_start(ap, q_g[:]),
                        waits=[tok_q], inc=dS, amt=16))
                if dbg_toks2:
                    t_dbg2 = p.add("vector", lambda v: v.engine_nop(),
                                   waits=dbg_toks2, inc=vec)
                else:
                    t_dbg2 = None

                ph2_done = [t_out_tok[NT - 1], heads_cons, headsF_cons,
                            last_scores, sc_cons[1], at_cons[0], at_cons[1],
                            av_cons[0], av_cons[1], tok_q, t_dbg2]

        with contextlib.ExitStack() as ph3:
            w1b = ph3.enter_context(nc.sbuf_tensor("w1b", [128, DC, FF], BF16))
            w2b = ph3.enter_context(nc.sbuf_tensor("w2b", [128, FC, D], BF16))
            g_sb = ph3.enter_context(
                nc.sbuf_tensor("g_sb", [128, FC, 1024], BF16))
            mwes = ph3.enter_context(nc.sbuf_tensor("mwes", [E + 1, S], BF16))
            bcst = ph3.enter_context(
                nc.sbuf_tensor("bcst", [128, 2, 1024], BF16))
            b2n = ph3.enter_context(nc.sbuf_tensor("b2n", [E + 1, D], BF16))
            mwes_d = nc.dram_tensor("mwes_scratch", [E + 1, S], BF16,
                                    kind="Internal").ap()

            t_b2 = p.add("gpsimd", lambda g: g.dma_start(
                b2n[0:E, :], b2_d), waits=ph2_done, inc=dG, amt=16)
            t_bo = p.add("gpsimd", lambda g: g.dma_start(
                b2n[E:E + 1, :], bo_d), inc=dG, amt=16)


            with contextlib.ExitStack() as ph3a:
                mesP = ph3a.enter_context(
                    nc.psum_tensor("mesP", [E + 1, NT, 128], BF16))

                def mtr(te):
                    for t in range(NT):
                        inst = te.transpose(mesP[:, t, :], maskw[:, t, :],
                                            ident[:])
                    return inst
                t_mes = p.add("tensor", mtr,
                              waits=list(ph2_done) + [t_mwone], inc=ten)
                t_mwes = p.add("vector", lambda v: v.tensor_copy(
                    mwes[0:E + 1, :], mesP[:]), waits=[t_mes], inc=vec)
            t_mwd = p.add("sync", lambda s: s.dma_start(mwes_d, mwes[:]),
                          waits=[t_mwes], inc=dS, amt=16)

            hidP = [ph3.enter_context(
                nc.psum_tensor(f"hidP{i}", [128, 2, 512], F32))
                for i in range(2)]
            outP = [ph3.enter_context(
                nc.psum_tensor(f"outP{i}", [128, 2, 512], F32))
                for i in range(2)]

            tok_b2e = [None] * NT
            for t in range(NT):
                w = [t_mwes, t_b2, t_bo, t_out_tok[t]] + \
                    (list(ph2_done) if t < 2 else [])
                if t >= 2:
                    w.append(tok_b2e[t - 2])

                def bmm(te, t=t):
                    pb = outP[t % 2]
                    for j in range(2):
                        inst = te.matmul(
                            pb[:, j, :],
                            mwes[0:E + 1, 128 * t:128 * t + 128],
                            b2n[:, 512 * j:512 * j + 512],
                            start=True, stop=True)
                    return inst
                t_bmm = p.add("tensor", bmm, waits=w, inc=ten)
                tok_b2e[t] = p.add(
                    "vector",
                    lambda v, t=t: v.tensor_add(
                        x_sb[:, t, :], x_sb[:, t, :],
                        bass.AP(outP[t % 2], 0, [[1024, 128], [1, 1024]])),
                    waits=[t_bmm], inc=vec)

            def load_w1(e, waits):
                return p.add("gpsimd", lambda g, e=e: g.dma_start(
                    w1b[:], w1_d[e].rearrange("(c p) f -> p c f", p=128)),
                    waits=waits, inc=dG, amt=16)

            def load_w2(e, waits):
                return p.add("gpsimd", lambda g, e=e: g.dma_start(
                    w2b[:], w2_d[e].rearrange("(c p) d -> p c d", p=128)),
                    waits=waits, inc=dG, amt=16)

            t_w1 = load_w1(0, ph2_done)
            t_w2 = load_w2(0, ph2_done)

            gel_tok = [None, None]
            out_evac = [None, None]

            def emit_bcast(pair, extra_w):
                e, hh = pair // 2, pair % 2
                slot = pair % 2
                return p.add("gpsimd", lambda g, e=e, hh=hh, slot=slot:
                             g.dma_start(
                                 bcst[:, slot, :],
                                 bass.AP(mwes_d.tensor, e * S + 1024 * hh,
                                         [[0, 128], [1, 1024]])),
                             waits=[t_mwd] + list(extra_w), inc=dG, amt=16)

            tok_bc = [None, None]
            tok_bc[0] = emit_bcast(0, list(ph2_done))
            tok_bc[1] = emit_bcast(1, list(ph2_done))
            if "ffn" in dbg:
                ap1 = dbg_out("mwes", [E + 1, S], BF16)
                ap3 = dbg_out("bcst", [128, 2, 1024], BF16)
                d1 = p.add("sync", lambda s, ap1=ap1: s.dma_start(
                    ap1, mwes[:]), waits=[t_mwes], inc=dS, amt=16)
                d3 = p.add("sync", lambda s, ap3=ap3: s.dma_start(
                    ap3, bcst[:]), waits=[tok_bc[1]], inc=dS, amt=16)
                tok_bc[0] = p.add("vector", lambda v: v.engine_nop(),
                                  waits=[d1, d3, tok_bc[0]], inc=vec)

            st2_last = None
            out_final = [None] * NT
            msk_last = None
            msk_tok = [None] * FC
            t_st1_last = None

            for e in range(E):
                eb = e % 2
                for hh in range(2):
                    for ft in range(FC):
                        b = ft % 2
                        w = [t_w1]
                        if e == 0 and hh == 0 and ft < 2:
                            w += list(ph2_done) + [t_mwes]
                        if gel_tok[b] is not None:
                            w.append(gel_tok[b])

                        def s1mm(te, ft=ft, hh=hh, b=b):
                            for ic in range(DC):
                                for j in range(2):
                                    inst = te.matmul(
                                        hidP[b][:, j, :],
                                        w1b[:, ic, 128 * ft:128 * ft + 128],
                                        xn_F[:, ic,
                                             1024 * hh + 512 * j:
                                             1024 * hh + 512 * j + 512],
                                        start=(ic == 0), stop=(ic == DC - 1))
                            return inst
                        t_s1 = p.add("tensor", s1mm, waits=w, inc=ten)
                        t_st1_last = t_s1
                        w_g = [t_s1, t_b1T]
                        if st2_last is not None:
                            w_g.append(st2_last)
                        gel_tok[b] = p.add(
                            "scalar",
                            lambda s, ft=ft, e=e, b=b: s.activation(
                                g_sb[:, ft, :],
                                bass.AP(hidP[b], 0, [[1024, 128], [1, 1024]]),
                                AF.Gelu, bias=b1T[:, ft, e:e + 1]),
                            waits=w_g, inc=sca)
                        slot = (2 * e + hh) % 2
                        msk_last = p.add(
                            "vector",
                            lambda v, ft=ft, slot=slot: v.tensor_mul(
                                g_sb[:, ft, :], g_sb[:, ft, :],
                                bcst[:, slot, :]),
                            waits=[gel_tok[b], tok_bc[slot]], inc=vec)
                        msk_tok[ft] = msk_last

                    if hh == 1 and e + 1 < E:
                        t_w1 = load_w1(e + 1, [t_st1_last])

                    for stl in range(8):
                        t_glob = 8 * hh + stl
                        b = stl % 2
                        for fc in range(FC):
                            w = [t_w2, msk_tok[fc]]
                            if fc == 0:
                                if e == 0 and hh == 0 and stl < 2:
                                    w.append(tok_b2e[14 + b])
                                if out_evac[b] is not None:
                                    w.append(out_evac[b])

                            def s2mm(te, stl=stl, b=b, fc=fc):
                                for j in range(2):
                                    inst = te.matmul(
                                        outP[b][:, j, :],
                                        g_sb[:, fc,
                                             128 * stl:128 * stl + 128],
                                        w2b[:, fc, 512 * j:512 * j + 512],
                                        start=(fc == 0), stop=(fc == FC - 1))
                                return inst
                            t_s2 = p.add("tensor", s2mm, waits=w,
                                         inc=(ten if fc == FC - 1 else None))
                        st2_last = t_s2
                        out_evac[b] = p.add(
                            "vector",
                            lambda v, t_glob=t_glob, b=b: v.tensor_add(
                                x_sb[:, t_glob, :], x_sb[:, t_glob, :],
                                bass.AP(outP[b], 0,
                                        [[1024, 128], [1, 1024]])),
                            waits=[t_s2], inc=vec)
                        if e == E - 1:
                            out_final[t_glob] = out_evac[b]

                    pair = 2 * e + hh
                    if pair + 2 < 2 * E:
                        tok_bc[pair % 2] = emit_bcast(pair + 2, [msk_last])

                if e + 1 < E:
                    t_w2 = load_w2(e + 1, [st2_last])

                if e == 0 and "g0" in dbg:
                    ap = dbg_out("g0", [128, FC, 1024], BF16)
                    dtk = p.add("sync", lambda s, ap=ap: s.dma_start(
                        ap, g_sb[:]), waits=[msk_last], inc=dS, amt=16)
                    st2_last = p.add("tensor", lambda te: te.nop(),
                                     waits=[dtk, st2_last], inc=ten)

            for i in range(4):
                w = [out_final[4 * i + j] for j in range(4)]
                p.add("sync",
                      lambda s, i=i: s.dma_start(
                          out_d.rearrange("(t p) d -> p t d", p=128)[
                              :, 4 * i:4 * i + 4, :],
                          x_sb[:, 4 * i:4 * i + 4, :]),
                      waits=w, inc=dS, amt=16)

            p.add("sync", lambda s: s.nop(), waits=[(dS, dS.n), (dS2, dS2.n)])
            p.add("gpsimd", lambda g: g.nop(), waits=[(dG, dG.n)])

        p.emit()

    return nc, dbg_aps


_CACHED = {}


def _get_program(dbg=()):
    key = tuple(sorted(dbg))
    if key not in _CACHED:
        _CACHED[key] = build_program(dbg)
    return _CACHED[key]


def make_in_maps(inputs):
    a = {k: np.ascontiguousarray(np.asarray(v, dtype=np.float32))
         for k, v in inputs.items()}
    x = a["L0_attn_x"]
    shared = {
        "Wq": a["L0_attn_Wq"], "Wk": a["L0_attn_Wk"], "Wv": a["L0_attn_Wv"],
        "Wo": a["L0_attn_Wo"], "bq": a["L0_attn_bq"], "bk": a["L0_attn_bk"],
        "bv": a["L0_attn_bv"].reshape(1, D),
        "bo": a["L0_attn_bo"].reshape(1, D),
        "rw": a["L0_ffn_router_w"], "w1": a["L0_ffn_w1"], "b1": a["L0_ffn_b1"],
        "w2": a["L0_ffn_w2"], "b2": a["L0_ffn_b2"],
    }
    return [dict(shared, x=np.ascontiguousarray(x[c])) for c in range(8)]


def kernel(L0_attn_x, L0_attn_Wq, L0_attn_bq, L0_attn_Wk, L0_attn_bk,
           L0_attn_Wv, L0_attn_bv, L0_attn_Wo, L0_attn_bo,
           L0_ffn_router_w, L0_ffn_w1, L0_ffn_b1, L0_ffn_w2, L0_ffn_b2):
    nc, _ = _get_program()
    in_maps = make_in_maps(dict(
        L0_attn_x=L0_attn_x, L0_attn_Wq=L0_attn_Wq, L0_attn_bq=L0_attn_bq,
        L0_attn_Wk=L0_attn_Wk, L0_attn_bk=L0_attn_bk, L0_attn_Wv=L0_attn_Wv,
        L0_attn_bv=L0_attn_bv, L0_attn_Wo=L0_attn_Wo, L0_attn_bo=L0_attn_bo,
        L0_ffn_router_w=L0_ffn_router_w, L0_ffn_w1=L0_ffn_w1,
        L0_ffn_b1=L0_ffn_b1, L0_ffn_w2=L0_ffn_w2, L0_ffn_b2=L0_ffn_b2))
    res = run_bass_kernel_spmd(nc, in_maps, core_ids=list(range(8)))
    out = np.stack([np.asarray(res.results[c]["out"]) for c in range(8)])
    return np.ascontiguousarray(out.astype(np.float32))


# version 08
# speedup vs baseline: 1.0026x; 1.0026x over previous
import contextlib

import numpy as np

import concourse.bass as bass
import concourse.mybir as mybir
from concourse.bass_utils import run_bass_kernel_spmd

F32 = mybir.dt.float32
BF16 = mybir.dt.bfloat16
AF = mybir.ActivationFunctionType
ALU = mybir.AluOpType
AX = mybir.AxisListType

S = 2048
D = 1024
H = 16
E = 14
FF = 2048
NT = S // 128
DC = D // 128
FC = FF // 128
EXP_SHIFT = -6.0

ENGINES = ("sync", "gpsimd", "tensor", "vector", "scalar")


class Sem:
    def __init__(self, nc, name):
        self.h = nc.alloc_semaphore(name)
        self.n = 0
        self.name = name


class Plan:

    def __init__(self, nc):
        self.nc = nc
        self.ops = {e: [] for e in ENGINES}

    def add(self, engine, fn, waits=(), inc=None, amt=1):
        ws = tuple(t for t in waits if t is not None)
        self.ops[engine].append((ws, fn, inc, amt))
        if inc is not None:
            inc.n += amt
            return (inc, inc.n)
        return None

    def emit(self):
        with self.nc.Block() as block:
            for name in ENGINES:
                ops = self.ops[name]
                if not ops:
                    continue

                def body(eng, ops=ops):
                    seen = {}
                    for waits, fn, inc, amt in ops:
                        for sem, val in waits:
                            if seen.get(sem.name, 0) < val:
                                eng.wait_ge(sem.h, val)
                                seen[sem.name] = val
                        inst = fn(eng)
                        if inc is not None:
                            inst.then_inc(inc.h, amt)

                getattr(block, name)(body)


def build_program(dbg=()):
    nc = bass.Bass("TRN2", target_bir_lowering=False,
                   detect_race_conditions=False,
                   dynamic_dma_scratch_size=16384)

    xd = nc.dram_tensor("x", [S, D], F32, kind="ExternalInput").ap()
    wq_d = nc.dram_tensor("Wq", [D, D], F32, kind="ExternalInput").ap()
    wk_d = nc.dram_tensor("Wk", [D, D], F32, kind="ExternalInput").ap()
    wv_d = nc.dram_tensor("Wv", [D, D], F32, kind="ExternalInput").ap()
    wo_d = nc.dram_tensor("Wo", [D, D], F32, kind="ExternalInput").ap()
    bq_d = nc.dram_tensor("bq", [D], F32, kind="ExternalInput").ap()
    bk_d = nc.dram_tensor("bk", [D], F32, kind="ExternalInput").ap()
    bv_d = nc.dram_tensor("bv", [1, D], F32, kind="ExternalInput").ap()
    bo_d = nc.dram_tensor("bo", [1, D], F32, kind="ExternalInput").ap()
    rw_d = nc.dram_tensor("rw", [D, E], F32, kind="ExternalInput").ap()
    w1_d = nc.dram_tensor("w1", [E, D, FF], F32, kind="ExternalInput").ap()
    b1_d = nc.dram_tensor("b1", [E, FF], F32, kind="ExternalInput").ap()
    w2_d = nc.dram_tensor("w2", [E, FF, D], F32, kind="ExternalInput").ap()
    b2_d = nc.dram_tensor("b2", [E, D], F32, kind="ExternalInput").ap()
    out_d = nc.dram_tensor("out", [S, D], F32, kind="ExternalOutput").ap()

    dbg_aps = {}

    def dbg_out(name, shape, dtype):
        ap = nc.dram_tensor("dbg_" + name, shape, dtype,
                            kind="ExternalOutput").ap()
        dbg_aps[name] = ap
        return ap

    p = Plan(nc)
    dS = Sem(nc, "dS")
    dS2 = Sem(nc, "dS2")
    dG = Sem(nc, "dG")
    ten = Sem(nc, "ten")
    vec = Sem(nc, "vec")
    sca = Sem(nc, "sca")
    gps = Sem(nc, "gps")

    ctx = contextlib.ExitStack()
    with ctx:
        x_sb = ctx.enter_context(nc.sbuf_tensor("x_sb", [128, NT, D], F32))
        xn_F = ctx.enter_context(nc.sbuf_tensor("xn_F", [128, DC, S], BF16))
        ident = ctx.enter_context(nc.sbuf_tensor("ident", [128, 128], BF16))
        trimT = ctx.enter_context(nc.sbuf_tensor("trimT", [128, 128], BF16))
        ones = ctx.enter_context(nc.sbuf_tensor("ones", [128, 128], BF16))
        maskw = ctx.enter_context(
            nc.sbuf_tensor("maskw", [128, NT, E + 1], BF16))
        bq8 = ctx.enter_context(nc.sbuf_tensor("bq8", [128, DC], F32))
        bk_s = ctx.enter_context(nc.sbuf_tensor("bk_s", [128, DC], F32))
        b1T = ctx.enter_context(nc.sbuf_tensor("b1T", [128, FC, E], BF16))
        eshift = ctx.enter_context(nc.sbuf_tensor("eshift", [128, 1], F32))


        p.add("gpsimd", lambda g: g.memset(ones[:], 1.0))
        p.add("gpsimd", lambda g: g.affine_select(
            ident[:], ones[:], [[1, 128]], ALU.is_equal, 0.0,
            channel_multiplier=-1))
        t_tri = p.add("gpsimd", lambda g: g.affine_select(
            trimT[:], ones[:], [[1, 128]], ALU.is_ge, 0.0,
            channel_multiplier=-1), inc=gps)
        t_esh = p.add("gpsimd", lambda g: g.memset(eshift[:], EXP_SHIFT),
                      inc=gps)

        t_x = []
        for i in range(4):
            eng, sem = (("sync", dS) if i % 2 == 0 else ("scalar", dS2))
            t_x.append(p.add(
                eng,
                lambda s, i=i: s.dma_start(
                    x_sb[:, 4 * i:4 * i + 4, :],
                    xd.rearrange("(t p) d -> p t d", p=128)[:, 4 * i:4 * i + 4, :]),
                inc=sem, amt=16))

        t_bq = p.add("sync", lambda s: s.dma_start(
            bq8[:], bass.AP(bq_d.tensor, 0, [[1, 128], [128, DC]]),
            allow_slow_non_contiguous=True), inc=dS, amt=16)
        t_bk = p.add("sync", lambda s: s.dma_start(
            bk_s[:], bass.AP(bk_d.tensor, 0, [[1, 128], [128, DC]]),
            allow_slow_non_contiguous=True), inc=dS, amt=16)
        t_bq8 = p.add("vector", lambda v: v.tensor_scalar_mul(
            bq8[:], bq8[:], 0.125), waits=[t_bq], inc=vec)

        wsrc = [wk_d, wv_d, wq_d, wo_d]
        n_wld = 0
        wld_tokens = {}

        def plan_wload(war_toks):
            nonlocal n_wld
            i = n_wld
            mi, oc = i // DC, i % DC
            buf = i % 2
            n_wld += 1
            tok = p.add("gpsimd", lambda g, mi=mi, oc=oc, buf=buf: g.dma_start(
                wtmp[:, buf, :],
                wsrc[mi].rearrange("(c p) i -> p c i", p=128)[:, oc, :]),
                waits=list(war_toks), inc=dG, amt=16)
            wld_tokens[(mi, oc)] = (tok, buf)
            return tok


        with contextlib.ExitStack() as ph1:
            xn_t = ph1.enter_context(nc.sbuf_tensor("xn_t", [128, 4, D], F32))
            hi_t = ph1.enter_context(nc.sbuf_tensor("hi_t", [128, 4, D], BF16))
            lo_t = ph1.enter_context(nc.sbuf_tensor("lo_t", [128, 4, D], BF16))
            lo_F = ph1.enter_context(
                nc.sbuf_tensor("lo_F", [128, 4, DC, 128], BF16))
            bnst = ph1.enter_context(
                nc.sbuf_tensor("bnst", [128, 4, 12], F32))
            st1 = ph1.enter_context(nc.sbuf_tensor("st1", [128, NT, 8], F32))
            logit = ph1.enter_context(
                nc.sbuf_tensor("logit", [128, NT, E], F32))
            expl = ph1.enter_context(nc.sbuf_tensor("expl", [128, 4, E], F32))
            rw_f = ph1.enter_context(nc.sbuf_tensor("rw_f", [128, DC, E], F32))
            rhl = ph1.enter_context(
                nc.sbuf_tensor("rhl", [128, DC, 2 * E], BF16))
            b1n = ph1.enter_context(nc.sbuf_tensor("b1n", [E, FF], BF16))

            trPh = ph1.enter_context(
                nc.psum_tensor("trPh", [128, DC, 128], BF16))
            trPl = ph1.enter_context(
                nc.psum_tensor("trPl", [128, DC, 128], BF16))
            rp1 = ph1.enter_context(nc.psum_tensor("rp1", [128, E], F32))
            b1tP = ph1.enter_context(
                nc.psum_tensor("b1tP", [128, FC, E], BF16))

            t_rw = p.add("sync", lambda s: s.dma_start(
                rw_f[:], rw_d.rearrange("(c p) e -> p c e", p=128),
                allow_slow_non_contiguous=True), inc=dS, amt=16)
            p.add("vector", lambda v: v.tensor_copy(rhl[:, :, 0:E], rw_f[:]),
                  waits=[t_rw])
            for _ in range(3):
                p.add("vector", lambda v: v.engine_nop())
            t_rlo = p.add("vector", lambda v: v.tensor_tensor(
                rhl[:, :, E:2 * E], rw_f[:], rhl[:, :, 0:E], ALU.subtract),
                inc=vec)
            t_b1 = p.add("gpsimd", lambda g: g.dma_start(b1n[:], b1_d),
                         inc=dG, amt=16)

            def b1tr(te):
                for j in range(FC):
                    inst = te.transpose(
                        b1tP[:, j, :], b1n[0:E, 128 * j:128 * j + 128],
                        ident[0:E, 0:E])
                return inst
            t_b1t = p.add("tensor", b1tr, waits=[t_b1, t_tri], inc=ten)
            t_b1T = p.add("vector", lambda v: v.tensor_copy(b1T[:], b1tP[:]),
                          waits=[t_b1t], inc=vec)

            tok_xn = [None] * NT
            tok_route = [None] * NT
            tok_loF = [None] * NT
            trtok = [None] * NT
            last1_sca = None
            for t4 in range(NT // 4):
                tiles = [4 * t4 + i for i in range(4)]
                for t in tiles:
                    p.add("vector", lambda v, t=t: v.bn_stats(
                        bnst[:, t % 4, 0:6], x_sb[:, t, 0:512]),
                        waits=[t_x[t // 4]])
                for t in tiles:
                    p.add("vector", lambda v, t=t: v.bn_stats(
                        bnst[:, t % 4, 6:12], x_sb[:, t, 512:1024]))
                for t in tiles:
                    p.add("vector", lambda v, t=t: v.bn_aggr(
                        st1[:, t, 2:4], bnst[:, t % 4, :]))
                for t in tiles:
                    p.add("vector", lambda v, t=t: v.tensor_scalar_add(
                        st1[:, t, 3:4], st1[:, t, 3:4], 1e-5))
                t_rcs = {}
                for t in tiles:
                    t_rcs[t] = p.add("vector", lambda v, t=t: v.reciprocal(
                        st1[:, t, 5:6], st1[:, t, 3:4]), inc=vec)
                t_rss = {}
                for t in tiles:
                    t_rss[t] = p.add("scalar", lambda s, t=t: s.activation(
                        st1[:, t, 6:7], st1[:, t, 5:6], AF.Sqrt),
                        waits=[t_rcs[t]], inc=sca)
                t_bls = {}
                for t in tiles:
                    t_bls[t] = p.add(
                        "vector", lambda v, t=t: v.scalar_tensor_tensor(
                            st1[:, t, 7:8], st1[:, t, 2:3], -1.0,
                            st1[:, t, 6:7], ALU.mult, ALU.mult),
                        waits=[t_rss[t]], inc=vec)
                t_xns = {}
                for t in tiles:
                    t_xns[t] = p.add(
                        "scalar", lambda s, t=t: s.activation(
                            xn_t[:, t % 4, :], x_sb[:, t, :], AF.Identity,
                            bias=st1[:, t, 7:8], scale=st1[:, t, 6:7]),
                        waits=[t_bls[t]], inc=sca)
                    last1_sca = t_xns[t]
                t_his = {}
                for t in tiles:
                    w_hi = [t_xns[t]]
                    if t >= 4:
                        w_hi.append(trtok[t - 4])
                    t_his[t] = p.add("vector", lambda v, t=t: v.tensor_copy(
                        hi_t[:, t % 4, :], xn_t[:, t % 4, :]),
                        waits=w_hi, inc=vec)
                t_los = {}
                for t in tiles:
                    t_los[t] = p.add("vector", lambda v, t=t: v.tensor_tensor(
                        lo_t[:, t % 4, :], xn_t[:, t % 4, :],
                        hi_t[:, t % 4, :], ALU.subtract), inc=vec)

                for t in tiles:
                    b = t % 4
                    w_tr = [t_his[t], t_los[t], t_tri]
                    if t >= 4:
                        w_tr += [tok_xn[t - 4], tok_loF[t - 4]]

                    def tr_hi(te, b=b):
                        for c in range(DC):
                            inst = te.transpose(
                                trPh[:, c, :],
                                hi_t[:, b, 128 * c:128 * c + 128], ident[:])
                        return inst
                    t_trh = p.add("tensor", tr_hi, waits=w_tr, inc=ten)

                    def tr_lo(te, b=b):
                        for c in range(DC):
                            inst = te.transpose(
                                trPl[:, c, :],
                                lo_t[:, b, 128 * c:128 * c + 128], ident[:])
                        return inst
                    t_trl = p.add("tensor", tr_lo, inc=ten)
                    trtok[t] = t_trl
                    tok_xn[t] = p.add("vector", lambda v, t=t: v.tensor_copy(
                        bass.AP(xn_F, 128 * t,
                                [[DC * S, 128], [S, DC], [1, 128]]),
                        trPh[:]), waits=[t_trh], inc=vec)
                    tok_loF[t] = p.add("vector", lambda v, b=b: v.tensor_copy(
                        lo_F[:, b, :, :], trPl[:]), waits=[t_trl],
                        inc=vec)

                    w_rt = [tok_xn[t], tok_loF[t], t_rlo]
                    if t >= 1:
                        w_rt.append(tok_route[t - 1])

                    def rmm(te, t=t, b=b):
                        for half in range(2):
                            for c in range(DC):
                                te.matmul(
                                    rp1[:],
                                    xn_F[:, c, 128 * t:128 * t + 128],
                                    rhl[:, c, E * half:E * half + E],
                                    start=(half == 0 and c == 0), stop=False)
                        for half in range(2):
                            for c in range(DC):
                                inst = te.matmul(
                                    rp1[:], lo_F[:, b, c, :],
                                    rhl[:, c, E * half:E * half + E],
                                    start=False,
                                    stop=(half == 1 and c == DC - 1))
                        return inst
                    t_rmm = p.add("tensor", rmm, waits=w_rt, inc=ten)
                    tok_route[t] = p.add(
                        "vector", lambda v, t=t: v.tensor_copy(
                            logit[:, t, :], rp1[:]),
                        waits=[t_rmm], inc=vec)

            t_mwone = p.add("vector", lambda v: v.memset(
                bass.AP(maskw, E, [[NT * (E + 1), 128], [E + 1, NT]]), 1.0),
                inc=vec)
            t_mw = None
            for t4 in range(NT // 4):
                tiles = [4 * t4 + i for i in range(4)]
                for t in tiles:
                    p.add("vector", lambda v, t=t: v.reduce_max(
                        st1[:, t, 0:1], logit[:, t, :], axis=AX.X),
                        waits=[tok_route[t]])
                t_nms = {}
                for t in tiles:
                    t_nms[t] = p.add("vector", lambda v, t=t: v.tensor_scalar_mul(
                        st1[:, t, 1:2], st1[:, t, 0:1], -1.0), inc=vec)
                t_exs = {}
                for t in tiles:
                    t_exs[t] = p.add("scalar", lambda s, t=t: s.activation(
                        expl[:, t % 4, :], logit[:, t, :], AF.Exp,
                        bias=st1[:, t, 1:2]), waits=[t_nms[t]], inc=sca)
                for t in tiles:
                    p.add("vector", lambda v, t=t: v.reduce_sum(
                        st1[:, t, 2:3], expl[:, t % 4, :], axis=AX.X),
                        waits=[t_exs[t]])
                for t in tiles:
                    p.add("vector", lambda v, t=t: v.reciprocal(
                        st1[:, t, 3:4], st1[:, t, 2:3]))
                for t in tiles:
                    t_mw = p.add("vector", lambda v, t=t: v.tensor_scalar(
                        maskw[:, t, 0:E], logit[:, t, :], st1[:, t, 0:1],
                        st1[:, t, 3:4], ALU.is_equal, ALU.mult), inc=vec)

            dbg_toks1 = []
            if "logits" in dbg:
                ap = dbg_out("logits", [128, NT, E], F32)
                dbg_toks1.append(p.add(
                    "sync", lambda s, ap=ap: s.dma_start(ap, logit[:]),
                    waits=[t_mw], inc=dS, amt=16))
            if "maskw" in dbg:
                ap = dbg_out("maskw", [128, NT, E], BF16)
                dbg_toks1.append(p.add(
                    "sync", lambda s, ap=ap: s.dma_start(
                        ap, maskw[:, :, 0:E]), waits=[t_mw], inc=dS, amt=16))
            if "xnf" in dbg:
                ap = dbg_out("xnf", [128, DC, S], BF16)
                dbg_toks1.append(p.add(
                    "sync", lambda s, ap=ap: s.dma_start(ap, xn_F[:]),
                    waits=[tok_xn[NT - 1]], inc=dS, amt=16))
            if dbg_toks1:
                t_dbg1 = p.add("vector", lambda v: v.engine_nop(),
                               waits=dbg_toks1, inc=vec)
            else:
                t_dbg1 = None

            ph1_done = [t_mw, last1_sca, t_rmm, t_b1T, t_dbg1]

        with contextlib.ExitStack() as ph2:
            wqo = ph2.enter_context(
                nc.sbuf_tensor("wqo", [128, 2 * DC, D], BF16))
            k_F = ph2.enter_context(nc.sbuf_tensor("k_F", [128, DC, S], BF16))
            v_tm = ph2.enter_context(
                nc.sbuf_tensor("v_tm", [128, NT, 65 * H], BF16))

            t_bv = p.add("gpsimd", lambda g: g.dma_start(
                wqo[0:1, 0, :], bv_d), waits=ph1_done, inc=dG, amt=16)

            with contextlib.ExitStack() as ph2w:
                wtmp = ph2w.enter_context(
                    nc.sbuf_tensor("wtmp", [128, 2, D], BF16))
                wkv = ph2w.enter_context(
                    nc.sbuf_tensor("wkv", [128, DC, 512], BF16))
                plan_wload(ph1_done)
                plan_wload(ph1_done)
                wtP = ph2w.enter_context(
                    nc.psum_tensor("wtP", [128, DC, 128], BF16))
                qkvP = [ph2w.enter_context(
                    nc.psum_tensor(f"qkvP{i}", [128, 512], F32))
                    for i in range(2)]

                def prep_WT(mi, dst_tensor, oc_base, dst_nc, extra_wait):
                    toks = []
                    t_tr = None
                    for oc in range(DC):
                        t_ld, buf = wld_tokens[(mi, oc)]
                        def wtr(te, buf=buf):
                            for ic in range(DC):
                                inst = te.transpose(
                                    wtP[:, ic, :],
                                    wtmp[:, buf, 128 * ic:128 * ic + 128],
                                    ident[:])
                            return inst
                        w_tr = [t_ld] + list(extra_wait)
                        if toks:
                            w_tr.append(toks[-1])
                        t_tr = p.add("tensor", wtr, waits=w_tr, inc=ten)
                        toks.append(p.add(
                            "vector",
                            lambda v, oc=oc: v.tensor_copy(
                                bass.AP(dst_tensor,
                                        oc_base * D + 128 * oc,
                                        [[dst_nc * D, 128], [D, DC],
                                         [1, 128]]),
                                wtP[:]),
                            waits=[t_tr], inc=vec))
                        if n_wld < 4 * DC:
                            plan_wload([t_tr])
                    return toks[-1], t_tr

                def prep_half(mi, half, extra_wait):
                    toks = []
                    t_tr = None
                    for oc in range(4 * half, 4 * half + 4):
                        t_ld, buf = wld_tokens[(mi, oc)]

                        def wtr(te, buf=buf):
                            for ic in range(DC):
                                inst = te.transpose(
                                    wtP[:, ic, :],
                                    wtmp[:, buf, 128 * ic:128 * ic + 128],
                                    ident[:])
                            return inst
                        w_tr = [t_ld] + list(extra_wait)
                        if toks:
                            w_tr.append(toks[-1])
                        t_tr = p.add("tensor", wtr, waits=w_tr, inc=ten)
                        toks.append(p.add(
                            "vector",
                            lambda v, oc=oc, half=half: v.tensor_copy(
                                bass.AP(wkv, 128 * (oc - 4 * half),
                                        [[DC * 512, 128], [512, DC],
                                         [1, 128]]),
                                wtP[:]),
                            waits=[t_tr], inc=vec))
                        if n_wld < 4 * DC:
                            plan_wload([t_tr])
                    return toks[-1], t_tr

                tok_k = []
                t_kmm = None
                for half in range(2):
                    ew = list(ph1_done) if half == 0 else [t_kmm]
                    t_wkT, _ = prep_half(0, half, ew)
                    for oc4 in range(4):
                        oc = 4 * half + oc4
                        for j4 in range(4):
                            buf = (oc * 4 + j4) % 2
                            w = [t_wkT, tok_xn[NT - 1], t_bk]
                            if len(tok_k) >= 2:
                                w.append(tok_k[-2])

                            def kmm(te, oc4=oc4, j4=j4, buf=buf):
                                for ic in range(DC):
                                    inst = te.matmul(
                                        qkvP[buf][:],
                                        wkv[:, ic, 128 * oc4:128 * oc4 + 128],
                                        xn_F[:, ic, 512 * j4:512 * j4 + 512],
                                        start=(ic == 0), stop=(ic == DC - 1))
                                return inst
                            t_kmm = p.add("tensor", kmm, waits=w, inc=ten)
                            tok_k.append(p.add(
                                "scalar",
                                lambda s, oc=oc, j4=j4, buf=buf: s.activation(
                                    k_F[:, oc, 512 * j4:512 * j4 + 512],
                                    qkvP[buf][:], AF.Identity,
                                    bias=bk_s[:, oc:oc + 1]),
                                waits=[t_kmm], inc=sca))
                t_kdone = tok_k[-1]

                t_ones_v = p.add("vector", lambda v: v.memset(
                    bass.AP(v_tm, 64,
                            [[NT * 65 * H, 128], [65 * H, NT], [65, H]]),
                    1.0), inc=vec)
                tok_v = []
                t_vmm = None
                for half in range(2):
                    oc2 = half
                    t_wvT, _ = prep_half(1, half, [t_kmm if half == 0 else t_vmm])
                    for st in range(NT):
                        buf = (st + 2 * half) % 2
                        w = [t_wvT, t_bv, tok_xn[NT - 1]]
                        if len(tok_v) >= 2:
                            w.append(tok_v[-2])
                        else:
                            w.append(tok_k[-1])

                        def vmm(te, st=st, oc2=oc2, buf=buf):
                            for ic in range(DC):
                                te.matmul(
                                    qkvP[buf][:],
                                    xn_F[:, ic, 128 * st:128 * st + 128],
                                    wkv[:, ic, :],
                                    start=(ic == 0), stop=False)
                            return te.matmul(
                                qkvP[buf][:], ones[0:1, :],
                                wqo[0:1, 0, 512 * oc2:512 * oc2 + 512],
                                start=False, stop=True)
                        t_vmm = p.add("tensor", vmm, waits=w, inc=ten)
                        tok_v.append(p.add(
                            "vector",
                            lambda v, st=st, oc2=oc2, buf=buf: v.tensor_copy(
                                bass.AP(v_tm, st * 65 * H + oc2 * 65 * 8,
                                        [[NT * 65 * H, 128], [65, 8],
                                         [1, 64]]),
                                qkvP[buf][:]),
                            waits=[t_vmm], inc=vec))
                t_vdone = tok_v[-1]

                t_wqT, _ = prep_WT(2, wqo, 0, 2 * DC, [t_vmm])
                t_woT, t_wo_tr = prep_WT(3, wqo, DC, 2 * DC, [t_vmm, t_wqT])
                ph2w_done = [t_kdone, t_vdone, t_wqT, t_woT, t_wo_tr,
                             tok_v[-2]]

            with contextlib.ExitStack() as ph2b:
                q_g = ph2b.enter_context(
                    nc.sbuf_tensor("q_g", [128, DC, 128], BF16))
                attnT = ph2b.enter_context(
                    nc.sbuf_tensor("attnT", [128, 2, NT, 128], BF16))
                heads = ph2b.enter_context(
                    nc.sbuf_tensor("heads", [128, D], BF16))
                inv_s = ph2b.enter_context(
                    nc.sbuf_tensor("inv_s", [128, H], F32))
                rs_t = ph2b.enter_context(
                    nc.sbuf_tensor("rs_t", [128, H], F32))

                scP = [ph2b.enter_context(
                    nc.psum_tensor(f"scP{i}", [128, 8, 128], F32))
                    for i in range(2)]
                avP = ph2b.enter_context(
                    nc.psum_tensor("avP", [128, 65], F32))
                trP = ph2b.enter_context(
                    nc.psum_tensor("trPa", [128, DC, 128], BF16))
                woP = ph2b.enter_context(
                    nc.psum_tensor("woP", [128, 2, 512], F32))

                q_last_evac = [None]

                def emit_qgroup(qt, w_extra):
                    toks = []
                    for oc in range(DC):
                        w = [t_wqT, tok_xn[NT - 1], t_bq8] + list(w_extra)
                        if toks:
                            w = [toks[-1]]
                        elif q_last_evac[0] is not None:
                            w.append(q_last_evac[0])

                        def qmm(te, oc=oc, qt=qt):
                            for ic in range(DC):
                                inst = te.matmul(
                                    bass.AP(woP, 0, [[1024, 128], [1, 128]]),
                                    wqo[:, ic, 128 * oc:128 * oc + 128],
                                    xn_F[:, ic, 128 * qt:128 * qt + 128],
                                    start=(ic == 0), stop=(ic == DC - 1))
                            return inst
                        t_mmq = p.add("tensor", qmm, waits=w, inc=ten)
                        toks.append(p.add(
                            "scalar",
                            lambda s, oc=oc: s.activation(
                                q_g[:, oc, :],
                                bass.AP(woP, 0, [[1024, 128], [1, 128]]),
                                AF.Identity,
                                bias=bq8[:, oc:oc + 1], scale=0.125),
                            waits=[t_mmq], inc=sca))
                    q_last_evac[0] = toks[-1]
                    return toks[-1]

                tok_q = emit_qgroup(0, ph2w_done)

                sc_cons = [None, None]
                av_cons = [None, None]
                at_cons = [None, None]
                heads_cons = None
                headsF_cons = None
                grp_ctr = 0
                t_out_tok = [None] * NT
                last_scores = None

                def emit_av(pt, h, t_msk):
                    nch = pt + 1
                    ab = h % 2
                    w_av = [t_msk]
                    if av_cons[0] is not None:
                        w_av.append(av_cons[0])

                    def avmm(te, h=h, nch=nch, ab=ab):
                        for c2 in range(nch):
                            inst = te.matmul(
                                avP[:], attnT[:, ab, c2, :],
                                v_tm[:, c2, 65 * h:65 * h + 65],
                                start=(c2 == 0), stop=(c2 == nch - 1))
                        return inst
                    t_avmm = p.add("tensor", avmm, waits=w_av, inc=ten)
                    at_cons[ab] = t_avmm
                    w_ev = [t_avmm]
                    if h == 0 and headsF_cons[0] is not None:
                        w_ev.append(headsF_cons[0])
                    p.add("vector", lambda v, h=h: v.tensor_copy(
                        rs_t[:, h:h + 1], avP[:, 64:65]),
                        waits=w_ev)
                    av_cons[0] = p.add(
                        "vector",
                        lambda v, h=h: v.tensor_copy(
                            heads[:, 64 * h:64 * h + 64],
                            avP[:, 0:64]),
                        inc=vec)
                    return t_avmm

                def emit_tile_end(t):
                    p.add("vector", lambda v: v.engine_nop())
                    p.add("vector", lambda v: v.engine_nop())
                    p.add("vector", lambda v: v.reciprocal(
                        inv_s[:], rs_t[:]))
                    p.add("vector", lambda v: v.engine_nop())
                    p.add("vector", lambda v: v.engine_nop())
                    p.add("vector", lambda v: v.engine_nop())
                    t_heads = p.add(
                        "vector", lambda v: v.tensor_tensor(
                            heads[:], heads[:],
                            bass.AP(inv_s, 0, [[H, 128], [1, H], [0, 64]]),
                            ALU.mult),
                        inc=vec)
                    if t == 0 and "heads0" in dbg:
                        ap = dbg_out("heads0", [128, D], BF16)
                        tk = p.add("sync", lambda s, ap=ap: s.dma_start(
                            ap, heads[:]), waits=[t_heads], inc=dS, amt=16)
                        t_heads = p.add("vector", lambda v: v.engine_nop(),
                                        waits=[tk, t_heads], inc=vec)
                    w_tr = [t_heads]
                    if headsF_cons[0] is not None:
                        w_tr.append(headsF_cons[0])

                    def htr(te):
                        for oc in range(DC):
                            inst = te.transpose(
                                trP[:, oc, :],
                                heads[:, 128 * oc:128 * oc + 128], ident[:])
                        return inst
                    heads_cons[0] = p.add("tensor", htr, waits=w_tr, inc=ten)
                    t_hF = p.add("vector", lambda v: v.tensor_copy(
                        heads[:], trP[:]), waits=[heads_cons[0]], inc=vec)

                    w_wo = [t_hF, t_woT]
                    if q_last_evac[0] is not None:
                        w_wo.append(q_last_evac[0])

                    def womm(te):
                        for oc in range(DC):
                            for j in range(2):
                                inst = te.matmul(
                                    woP[:, j, :],
                                    heads[:, 128 * oc:128 * oc + 128],
                                    wqo[:, DC + oc, 512 * j:512 * j + 512],
                                    start=(oc == 0), stop=(oc == DC - 1))
                        return inst
                    headsF_cons[0] = p.add("tensor", womm, waits=w_wo,
                                           inc=ten)
                    t_out_tok[t] = p.add(
                        "vector",
                        lambda v, t=t: v.tensor_add(
                            x_sb[:, t, :], x_sb[:, t, :],
                            bass.AP(woP, 0, [[1024, 128], [1, 1024]])),
                        waits=[headsF_cons[0]], inc=vec)

                heads_cons = [None]
                headsF_cons = [None]
                pend = None
                pend_end = None
                for t in range(NT):
                    g2 = t // 2
                    nch = t + 1
                    for h in range(H):
                        poff = 64 * (h % 2)
                        kc = h // 2
                        ngrp = (nch + 7) // 8
                        smm_toks = []
                        for gg in range(ngrp):
                            b = grp_ctr % 2
                            grp_ctr += 1
                            c0, c1 = 8 * gg, min(8 * gg + 8, nch)
                            w = [tok_q]
                            if sc_cons[b] is not None:
                                w.append(sc_cons[b])

                            def smm(te, b=b, c0=c0, c1=c1,
                                    poff=poff, kc=kc):
                                for c2 in range(c0, c1):
                                    inst = te.matmul(
                                        scP[b][:, c2 - c0, :],
                                        k_F[poff:poff + 64, kc,
                                            128 * c2:128 * c2 + 128],
                                        q_g[poff:poff + 64, kc, :],
                                        start=True, stop=True)
                                return inst
                            t_smm = p.add("tensor", smm, waits=w, inc=ten)
                            last_scores = t_smm
                            smm_toks.append((b, c0, c1, t_smm))

                        if pend_end is not None:
                            emit_tile_end(pend_end)
                            pend_end = None
                        if pend is not None:
                            emit_av(*pend)
                            if pend[1] == H - 1:
                                pend_end = pend[0]
                            pend = None

                        ab = h % 2
                        last_exp = None
                        for b, c0, c1, t_smm in smm_toks:
                            w_e = [t_smm]
                            if at_cons[ab] is not None:
                                w_e.append(at_cons[ab])

                            def eop(s, ab=ab, b=b, c0=c0, c1=c1):
                                n = c1 - c0
                                return s.activation(
                                    bass.AP(attnT,
                                            ab * NT * 128 + c0 * 128,
                                            [[2 * NT * 128, 128],
                                             [1, n * 128]]),
                                    bass.AP(scP[b], 0,
                                            [[8 * 128, 128], [1, n * 128]]),
                                    AF.Exp, bias=eshift[:])
                            sc_cons[b] = p.add("scalar", eop,
                                               waits=w_e + [t_esh], inc=sca)
                            last_exp = sc_cons[b]
                        t_msk = p.add(
                            "vector", lambda v, t=t, ab=ab: v.tensor_mul(
                                attnT[:, ab, t, :], attnT[:, ab, t, :],
                                trimT[:]),
                            waits=[last_exp], inc=vec)
                        if t == 0 and h == 0 and "attnT00" in dbg:
                            ap = dbg_out("attnT00", [128, 128], BF16)
                            tk = p.add("sync",
                                       lambda s, ap=ap: s.dma_start(
                                           ap, attnT[:, 0, 0, :]),
                                       waits=[t_msk], inc=dS, amt=16)
                            t_msk = p.add("vector",
                                          lambda v: v.engine_nop(),
                                          waits=[tk, t_msk], inc=vec)
                        pend = (t, h, t_msk)

                    if t < NT - 1:
                        w_q = [tk for tk in (t_out_tok[t - 1] if t >= 1
                                             else None,) if tk]
                        tok_q = emit_qgroup(t + 1, w_q)

                if pend_end is not None:
                    emit_tile_end(pend_end)
                emit_av(*pend)
                emit_tile_end(NT - 1)
                heads_cons = heads_cons[0]
                headsF_cons = headsF_cons[0]

                dbg_toks2 = []
                dbg_att = {}
                if "attn_out" in dbg:
                    ap = dbg_out("attn_out", [128, NT, D], F32)
                    dbg_toks2.append(p.add(
                        "sync", lambda s, ap=ap: s.dma_start(ap, x_sb[:]),
                        waits=[t_out_tok[NT - 1]], inc=dS, amt=16))
                if "kf" in dbg:
                    ap = dbg_out("kf", [128, DC, S], BF16)
                    dbg_toks2.append(p.add(
                        "sync", lambda s, ap=ap: s.dma_start(ap, k_F[:]),
                        waits=[t_kdone], inc=dS, amt=16))
                if "vtm" in dbg:
                    ap = dbg_out("vtm", [128, NT, 65 * H], BF16)
                    dbg_toks2.append(p.add(
                        "sync", lambda s, ap=ap: s.dma_start(ap, v_tm[:]),
                        waits=[t_vdone, t_ones_v], inc=dS, amt=16))
                if "qg" in dbg:
                    ap = dbg_out("qg", [128, DC, 128], BF16)
                    dbg_toks2.append(p.add(
                        "sync", lambda s, ap=ap: s.dma_start(ap, q_g[:]),
                        waits=[tok_q], inc=dS, amt=16))
                if dbg_toks2:
                    t_dbg2 = p.add("vector", lambda v: v.engine_nop(),
                                   waits=dbg_toks2, inc=vec)
                else:
                    t_dbg2 = None

                ph2_done = [t_out_tok[NT - 1], heads_cons, headsF_cons,
                            last_scores, sc_cons[1], at_cons[0], at_cons[1],
                            av_cons[0], av_cons[1], tok_q, t_dbg2]

        with contextlib.ExitStack() as ph3:
            w1b = ph3.enter_context(nc.sbuf_tensor("w1b", [128, DC, FF], BF16))
            w2b = ph3.enter_context(nc.sbuf_tensor("w2b", [128, FC, D], BF16))
            g_sb = ph3.enter_context(
                nc.sbuf_tensor("g_sb", [128, FC, 1024], BF16))
            mwes = ph3.enter_context(nc.sbuf_tensor("mwes", [E + 1, S], BF16))
            bcst = ph3.enter_context(
                nc.sbuf_tensor("bcst", [128, 2, 1024], BF16))
            b2n = ph3.enter_context(nc.sbuf_tensor("b2n", [E + 1, D], BF16))
            mwes_d = nc.dram_tensor("mwes_scratch", [E + 1, S], BF16,
                                    kind="Internal").ap()

            t_b2 = p.add("gpsimd", lambda g: g.dma_start(
                b2n[0:E, :], b2_d), waits=ph2_done, inc=dG, amt=16)
            t_bo = p.add("gpsimd", lambda g: g.dma_start(
                b2n[E:E + 1, :], bo_d), inc=dG, amt=16)


            with contextlib.ExitStack() as ph3a:
                mesP = ph3a.enter_context(
                    nc.psum_tensor("mesP", [E + 1, NT, 128], BF16))

                def mtr(te):
                    for t in range(NT):
                        inst = te.transpose(mesP[:, t, :], maskw[:, t, :],
                                            ident[:])
                    return inst
                t_mes = p.add("tensor", mtr,
                              waits=list(ph2_done) + [t_mwone], inc=ten)
                t_mwes = p.add("vector", lambda v: v.tensor_copy(
                    mwes[0:E + 1, :], mesP[:]), waits=[t_mes], inc=vec)
            t_mwd = p.add("sync", lambda s: s.dma_start(mwes_d, mwes[:]),
                          waits=[t_mwes], inc=dS, amt=16)

            hidP = [ph3.enter_context(
                nc.psum_tensor(f"hidP{i}", [128, 2, 512], F32))
                for i in range(2)]
            outP = [ph3.enter_context(
                nc.psum_tensor(f"outP{i}", [128, 2, 512], F32))
                for i in range(2)]

            tok_b2e = [None] * NT
            for t in range(NT):
                w = [t_mwes, t_b2, t_bo, t_out_tok[t]] + \
                    (list(ph2_done) if t < 2 else [])
                if t >= 2:
                    w.append(tok_b2e[t - 2])

                def bmm(te, t=t):
                    pb = outP[t % 2]
                    for j in range(2):
                        inst = te.matmul(
                            pb[:, j, :],
                            mwes[0:E + 1, 128 * t:128 * t + 128],
                            b2n[:, 512 * j:512 * j + 512],
                            start=True, stop=True)
                    return inst
                t_bmm = p.add("tensor", bmm, waits=w, inc=ten)
                tok_b2e[t] = p.add(
                    "vector",
                    lambda v, t=t: v.tensor_add(
                        x_sb[:, t, :], x_sb[:, t, :],
                        bass.AP(outP[t % 2], 0, [[1024, 128], [1, 1024]])),
                    waits=[t_bmm], inc=vec)

            def load_w1(e, waits):
                return p.add("gpsimd", lambda g, e=e: g.dma_start(
                    w1b[:], w1_d[e].rearrange("(c p) f -> p c f", p=128)),
                    waits=waits, inc=dG, amt=16)

            def load_w2(e, waits):
                return p.add("gpsimd", lambda g, e=e: g.dma_start(
                    w2b[:], w2_d[e].rearrange("(c p) d -> p c d", p=128)),
                    waits=waits, inc=dG, amt=16)

            t_w1 = load_w1(0, ph2_done)
            t_w2 = load_w2(0, ph2_done)

            gel_tok = [None, None]
            out_evac = [None, None]

            def emit_bcast(pair, extra_w):
                e, hh = pair // 2, pair % 2
                slot = pair % 2
                return p.add("gpsimd", lambda g, e=e, hh=hh, slot=slot:
                             g.dma_start(
                                 bcst[:, slot, :],
                                 bass.AP(mwes_d.tensor, e * S + 1024 * hh,
                                         [[0, 128], [1, 1024]])),
                             waits=[t_mwd] + list(extra_w), inc=dG, amt=16)

            tok_bc = [None, None]
            tok_bc[0] = emit_bcast(0, list(ph2_done))
            tok_bc[1] = emit_bcast(1, list(ph2_done))
            if "ffn" in dbg:
                ap1 = dbg_out("mwes", [E + 1, S], BF16)
                ap3 = dbg_out("bcst", [128, 2, 1024], BF16)
                d1 = p.add("sync", lambda s, ap1=ap1: s.dma_start(
                    ap1, mwes[:]), waits=[t_mwes], inc=dS, amt=16)
                d3 = p.add("sync", lambda s, ap3=ap3: s.dma_start(
                    ap3, bcst[:]), waits=[tok_bc[1]], inc=dS, amt=16)
                tok_bc[0] = p.add("vector", lambda v: v.engine_nop(),
                                  waits=[d1, d3, tok_bc[0]], inc=vec)

            st2_last = None
            out_final = [None] * NT
            msk_last = None
            msk_tok = [None] * FC
            t_st1_last = None

            for e in range(E):
                eb = e % 2
                for hh in range(2):
                    for ft in range(FC):
                        b = ft % 2
                        w = [t_w1]
                        if e == 0 and hh == 0 and ft < 2:
                            w += list(ph2_done) + [t_mwes]
                        if gel_tok[b] is not None:
                            w.append(gel_tok[b])

                        def s1mm(te, ft=ft, hh=hh, b=b):
                            for ic in range(DC):
                                for j in range(2):
                                    inst = te.matmul(
                                        hidP[b][:, j, :],
                                        w1b[:, ic, 128 * ft:128 * ft + 128],
                                        xn_F[:, ic,
                                             1024 * hh + 512 * j:
                                             1024 * hh + 512 * j + 512],
                                        start=(ic == 0), stop=(ic == DC - 1))
                            return inst
                        t_s1 = p.add("tensor", s1mm, waits=w, inc=ten)
                        t_st1_last = t_s1
                        w_g = [t_s1, t_b1T]
                        if st2_last is not None:
                            w_g.append(st2_last)
                        gel_tok[b] = p.add(
                            "scalar",
                            lambda s, ft=ft, e=e, b=b: s.activation(
                                g_sb[:, ft, :],
                                bass.AP(hidP[b], 0, [[1024, 128], [1, 1024]]),
                                AF.Gelu, bias=b1T[:, ft, e:e + 1]),
                            waits=w_g, inc=sca)
                        slot = (2 * e + hh) % 2
                        msk_last = p.add(
                            "vector",
                            lambda v, ft=ft, slot=slot: v.tensor_mul(
                                g_sb[:, ft, :], g_sb[:, ft, :],
                                bcst[:, slot, :]),
                            waits=[gel_tok[b], tok_bc[slot]], inc=vec)
                        msk_tok[ft] = msk_last

                    if hh == 1 and e + 1 < E:
                        t_w1 = load_w1(e + 1, [t_st1_last])

                    for stl in range(8):
                        t_glob = 8 * hh + stl
                        b = stl % 2
                        for fc in range(FC):
                            w = [t_w2, msk_tok[fc]]
                            if fc == 0:
                                if e == 0 and hh == 0 and stl < 2:
                                    w.append(tok_b2e[14 + b])
                                if out_evac[b] is not None:
                                    w.append(out_evac[b])

                            def s2mm(te, stl=stl, b=b, fc=fc):
                                for j in range(2):
                                    inst = te.matmul(
                                        outP[b][:, j, :],
                                        g_sb[:, fc,
                                             128 * stl:128 * stl + 128],
                                        w2b[:, fc, 512 * j:512 * j + 512],
                                        start=(fc == 0), stop=(fc == FC - 1))
                                return inst
                            t_s2 = p.add("tensor", s2mm, waits=w,
                                         inc=(ten if fc == FC - 1 else None))
                        st2_last = t_s2
                        out_evac[b] = p.add(
                            "vector",
                            lambda v, t_glob=t_glob, b=b: v.tensor_add(
                                x_sb[:, t_glob, :], x_sb[:, t_glob, :],
                                bass.AP(outP[b], 0,
                                        [[1024, 128], [1, 1024]])),
                            waits=[t_s2], inc=vec)
                        if e == E - 1:
                            out_final[t_glob] = out_evac[b]

                    pair = 2 * e + hh
                    if pair + 2 < 2 * E:
                        tok_bc[pair % 2] = emit_bcast(pair + 2, [msk_last])

                if e + 1 < E:
                    t_w2 = load_w2(e + 1, [st2_last])

                if e == 0 and "g0" in dbg:
                    ap = dbg_out("g0", [128, FC, 1024], BF16)
                    dtk = p.add("sync", lambda s, ap=ap: s.dma_start(
                        ap, g_sb[:]), waits=[msk_last], inc=dS, amt=16)
                    st2_last = p.add("tensor", lambda te: te.nop(),
                                     waits=[dtk, st2_last], inc=ten)

            for i in range(4):
                w = [out_final[4 * i + j] for j in range(4)]
                eng, sem = (("sync", dS) if i % 2 == 0 else ("scalar", dS2))
                p.add(eng,
                      lambda s, i=i: s.dma_start(
                          out_d.rearrange("(t p) d -> p t d", p=128)[
                              :, 4 * i:4 * i + 4, :],
                          x_sb[:, 4 * i:4 * i + 4, :]),
                      waits=w, inc=sem, amt=16)

            p.add("sync", lambda s: s.nop(), waits=[(dS, dS.n), (dS2, dS2.n)])
            p.add("gpsimd", lambda g: g.nop(), waits=[(dG, dG.n)])

        p.emit()

    return nc, dbg_aps


_CACHED = {}


def _get_program(dbg=()):
    key = tuple(sorted(dbg))
    if key not in _CACHED:
        _CACHED[key] = build_program(dbg)
    return _CACHED[key]


def make_in_maps(inputs):
    a = {k: np.ascontiguousarray(np.asarray(v, dtype=np.float32))
         for k, v in inputs.items()}
    x = a["L0_attn_x"]
    shared = {
        "Wq": a["L0_attn_Wq"], "Wk": a["L0_attn_Wk"], "Wv": a["L0_attn_Wv"],
        "Wo": a["L0_attn_Wo"], "bq": a["L0_attn_bq"], "bk": a["L0_attn_bk"],
        "bv": a["L0_attn_bv"].reshape(1, D),
        "bo": a["L0_attn_bo"].reshape(1, D),
        "rw": a["L0_ffn_router_w"], "w1": a["L0_ffn_w1"], "b1": a["L0_ffn_b1"],
        "w2": a["L0_ffn_w2"], "b2": a["L0_ffn_b2"],
    }
    return [dict(shared, x=np.ascontiguousarray(x[c])) for c in range(8)]


def kernel(L0_attn_x, L0_attn_Wq, L0_attn_bq, L0_attn_Wk, L0_attn_bk,
           L0_attn_Wv, L0_attn_bv, L0_attn_Wo, L0_attn_bo,
           L0_ffn_router_w, L0_ffn_w1, L0_ffn_b1, L0_ffn_w2, L0_ffn_b2):
    nc, _ = _get_program()
    in_maps = make_in_maps(dict(
        L0_attn_x=L0_attn_x, L0_attn_Wq=L0_attn_Wq, L0_attn_bq=L0_attn_bq,
        L0_attn_Wk=L0_attn_Wk, L0_attn_bk=L0_attn_bk, L0_attn_Wv=L0_attn_Wv,
        L0_attn_bv=L0_attn_bv, L0_attn_Wo=L0_attn_Wo, L0_attn_bo=L0_attn_bo,
        L0_ffn_router_w=L0_ffn_router_w, L0_ffn_w1=L0_ffn_w1,
        L0_ffn_b1=L0_ffn_b1, L0_ffn_w2=L0_ffn_w2, L0_ffn_b2=L0_ffn_b2))
    res = run_bass_kernel_spmd(nc, in_maps, core_ids=list(range(8)))
    out = np.stack([np.asarray(res.results[c]["out"]) for c in range(8)])
    return np.ascontiguousarray(out.astype(np.float32))
